# version 04 (final)
# baseline (speedup 1.0000x reference)
import contextlib
import numpy as np
import concourse.bass as bass
import concourse.mybir as mybir
from concourse.bass_utils import run_bass_kernel_spmd

F32 = mybir.dt.float32
BF16 = mybir.dt.bfloat16
ALU = mybir.AluOpType
AF = mybir.ActivationFunctionType
AX = mybir.AxisListType

S_TOK = 4096
D = 1024
DH = 4096
NST = 16
T_O = 256
T_I = 16
KAPPA = 0.5
ATTN_EPS = 1e-6


class Emit:
    ENG = ("t", "T", "v", "a", "g", "y")

    def __init__(self):
        self.q = {e: [] for e in self.ENG}
        self.c = {e: 0 for e in self.ENG}

    def op(self, e, fn, inc=0):
        if e == "t":
            self.c["t"] += inc
            self.c["T"] += inc
            self.q["t"].append((fn, inc))
            return self.c["t"]
        self.c[e] += inc
        self.q[e].append((fn, inc))
        return self.c[e]

    def wait(self, e, producer, value=None):
        v = self.c[producer] if value is None else value
        self.q[e].append((("W", producer, v), 0))

    def dr(self, e):
        self.q[e].append(("DRAIN", 0))

    def t(self, fn, inc=0): return self.op("t", fn, inc)
    def v(self, fn, inc=0): return self.op("v", fn, inc)
    def a(self, fn, inc=0): return self.op("a", fn, inc)
    def g(self, fn, inc=0): return self.op("g", fn, inc)
    def y(self, fn, inc=0): return self.op("y", fn, inc)

    def replay(self, e, eng, sems):
        sem = sems[e]
        strong_sem = sems["T"] if e == "t" else sem
        deferred = e in ("v", "a")
        pe = e == "t"
        pending = 0
        for fn, inc in self.q[e]:
            if isinstance(fn, tuple) and fn[0] == "W":
                if pending:
                    eng.drain().then_inc(strong_sem, pending)
                    pending = 0
                eng.wait_ge(sems[fn[1]], fn[2])
                continue
            if fn == "DRAIN":
                eng.drain()
                continue
            r = fn(eng)
            if deferred:
                if inc:
                    eng.drain().then_inc(sem, inc)
                else:
                    eng.drain()
            elif inc:
                if pe:
                    r.then_inc(sem, inc)
                    pending += inc
                else:
                    r.then_inc(sem, inc)
        if pending:
            eng.drain().then_inc(strong_sem, pending)


def TT(out, a, b, op=ALU.mult):
    return lambda e: e.tensor_tensor(out=out, in0=a, in1=b, op=op)

def TS(out, a, s1, op0, s2=None, op1=None):
    if op1 is None:
        return lambda e: e.tensor_scalar(out=out, in0=a, scalar1=s1, scalar2=None, op0=op0)
    return lambda e: e.tensor_scalar(out=out, in0=a, scalar1=s1, scalar2=s2, op0=op0, op1=op1)

def STT(out, a, s, b, op0, op1):
    return lambda e: e.scalar_tensor_tensor(out=out, in0=a, scalar=s, in1=b, op0=op0, op1=op1)

def RED(out, a, op=ALU.add):
    return lambda e: e.tensor_reduce(out=out, in_=a, axis=AX.X, op=op)

def REDA(out, a, op=ALU.max):
    return lambda e: e.tensor_reduce(out=out, in_=a, axis=AX.X, op=op, apply_absolute_value=True)

def CPY(out, a):
    return lambda e: e.tensor_copy(out=out, in_=a)

def ACTV(out, a, func, bias=None, scale=None):
    kw = {}
    if bias is not None: kw["bias"] = bias
    if scale is not None: kw["scale"] = scale
    return lambda e: e.activation(out=out, in_=a, func=func, **kw)

def RECIP(out, a):
    return lambda e: e.reciprocal(out=out, in_=a)

def MM(out, lhsT, rhs, start=True, stop=True):
    return lambda e: e.matmul(out, lhsT, rhs, start=start, stop=stop)

def TRN(out, a, ident):
    return lambda e: e.transpose(out, a, ident)

def DMA(out, in_):
    return lambda e: e.dma_start(out=out, in_=in_)

def DMAS(out, in_):
    return lambda e: e.dma_start(out=out, in_=in_, allow_slow_non_contiguous=True)

def MEMSET(ap, val):
    return lambda e: e.memset(ap, val)

def SCOPY(out, a):
    return lambda e: e.copy(out=out, in_=a)


def build(n_chunks=S_TOK // T_O, dbg=(), stop_after="full"):
    nc = bass.Bass("TRN2", target_bir_lowering=False)
    NCH = n_chunks
    dbg = set(dbg)

    din = {}
    def I(name, shape, dt=F32):
        din[name] = nc.dram_tensor(name, list(shape), dt, kind="ExternalInput").ap()
        return din[name]

    x_hbm = I("xb", (S_TOK, D))
    I("W_in", (D, 2 * D)); I("W_xproj", (D, 128)); I("W_dt", (64, D)); I("b_dt", (D,))
    I("W_out", (D, D)); I("b_out", (D,)); I("W_res", (D, 2 * NST)); I("A_log", (NST,))
    I("W_exp", (D, DH)); I("b_exp", (DH,)); I("up_bias", (DH,))
    I("W_con", (DH, D)); I("b_con", (D,)); I("down_bias", (D,))
    I("ln1_w", (D,)); I("ln1_b", (D,)); I("q_b", (64,)); I("k_b", (64,))
    I("v_b", (D,)); I("o_b", (D,)); I("ln2_w", (D,)); I("ln2_b", (D,))
    I("f1_b", (DH,)); I("f2_b", (D,))
    I("q_w", (64, D)); I("k_w", (64, D)); I("v_w", (D, D)); I("o_w", (D, D))
    I("f1_w", (DH, D)); I("f2_w", (D, DH)); I("w1", (DH, D)); I("w2", (D, DH))
    I("bfU", (12, 32, 2, 128, 128)); I("bfD", (12, 32, 2, 128, 128))
    I("ones128", (128, 128)); I("eye128", (128, 128)); I("onesK1", (1, 128))
    I("maskUT", (128, 128)); I("h0c", (128, 256)); I("m2c0", (128, 128)); I("hshift", (128, 32))

    out_hbm = nc.dram_tensor("out", [S_TOK, D], F32, kind="ExternalOutput").ap()
    dbg_h = {}

    scr = {}
    def SCR(name, shape, dt=F32):
        scr[name] = nc.dram_tensor(name, list(shape), dt).ap()
        return scr[name]
    SCR("vT", (D, D)); SCR("oT", (D, D)); SCR("f1T", (D, DH)); SCR("f2T", (DH, D))
    SCR("qkT", (D, 128)); SCR("w1c", (D, DH)); SCR("w2c", (DH, D)); SCR("b1c", (DH,))
    SCR("w1qT", (D, DH), BF16); SCR("w2qT", (DH, D), BF16)
    SCR("curA", (32, 128, D + 1)); SCR("curB", (32, 128, D + 1))
    SCR("cu2A", (32, 128, D)); SCR("cu2B", (32, 128, D))

    st = contextlib.ExitStack()
    SB = lambda name, shape, dt=F32: st.enter_context(nc.sbuf_tensor(name, list(shape), dt))

    W8 = 8 * T_O
    xR = SB("xR", (128, W8)); xN = SB("xN", (128, W8)); xI = SB("xI", (128, W8))
    zS = SB("zS", (128, W8)); dtS = SB("dtS", (128, W8)); dxS = SB("dxS", (128, W8))
    yS = SB("yS", (128, W8)); xT3 = SB("xT3", (128, W8)); xXX = SB("xXX", (128, W8))
    sc1 = SB("sc1", (128, W8)); sc2 = SB("sc2", (128, W8))
    wA = SB("wA", (128, 2048)); wB = SB("wB", (128, 2048))
    wAb = SB("wAb", (128, 4096), BF16); wBb = SB("wBb", (128, 4096), BF16)
    xpS = SB("xpS", (128, T_O)); xrS = SB("xrS", (32, T_O)); qkS = SB("qkS", (128, T_O))
    kvS = SB("kvS", (64, D + 1)); atm = SB("atm", (128, 128)); kSS = SB("kSS", (64, T_O))
    hA = SB("hA", (128, 256)); hB = SB("hB", (128, 256))
    gS = SB("gS", (128, T_I * 128)); bxS = SB("bxS", (128, T_I * 256))
    bcR = SB("bcR", (128, T_I * 96)); rxS = SB("rxS", (128, T_I * 32))
    flT = SB("flT", (T_I, 96)); flF = SB("flF", (1, T_I * 96))
    smA = SB("smA", (128, 64)); smB = SB("smB", (128, 64)); smC = SB("smC", (128, 64))
    f1a = SB("f1a", (128, 128)); f1b = SB("f1b", (128, 128)); f1c = SB("f1c", (128, 128))
    f1d = SB("f1d", (128, 128)); f1e = SB("f1e", (128, 128)); f1f = SB("f1f", (128, 128))
    m2cS = SB("m2cS", (128, 128)); m2qS = SB("m2qS", (128, 128)); s2S = SB("s2S", (128, 128))
    xTM = SB("xTM", (128, D))
    qbf = SB("qbf", (128, W8), BF16); hbf = SB("hbf", (128, DH), BF16)
    ones = SB("ones", (128, 128)); eye = SB("eye", (128, 128)); mUT = SB("mUT", (128, 128))
    hsh = SB("hsh", (128, 32))
    oK1 = SB("oK1", (1, 128)); aRep = SB("aRep", (128, NST))
    bias_dt = SB("bias_dt", (128, 8)); bias_out = SB("bias_out", (128, 8))
    bias_v = SB("bias_v", (128, 8)); bias_o = SB("bias_o", (128, 8))
    bias_qk = SB("bias_qk", (128, 1))
    bias_f1 = SB("bias_f1", (128, 32)); bias_f2 = SB("bias_f2", (128, 8))
    bias_b1 = SB("bias_b1", (128, 32)); bias_con = SB("bias_con", (128, 8))
    w_ln1 = SB("w_ln1", (128, 8)); b_ln1 = SB("b_ln1", (128, 8))
    w_ln2 = SB("w_ln2", (128, 8)); b_ln2 = SB("b_ln2", (128, 8))
    s1S = SB("s1S", (128, 1)); s2Sq = SB("s2Sq", (128, 1))
    stA = SB("stA", (128, T_O)); stB = SB("stB", (128, T_O))
    stC = SB("stC", (128, T_O)); stD = SB("stD", (128, T_O))
    scq = SB("scq", (128, 2)); scqh = SB("scqh", (128, 1))
    den_r = SB("den_r", (128, 128)); srow = SB("srow", (1, T_O))
    cst = SB("cst", (128, 4))
    scRep = SB("scRep", (128, T_O)); scRep2 = SB("scRep2", (128, 128))

    vTM = yS
    def hfm(m):
        sl = dtS if m < 16 else dxS
        mm_ = m % 16
        return sl[:, mm_ * 128:(mm_ + 1) * 128]
    def htm(seg0, seg1):
        return (xI[:, seg0:seg1], zS[:, seg0:seg1])

    PB = [st.enter_context(nc.psum_tensor(f"PB{i}", [128, 512], F32)) for i in range(8)]
    sems = {e: st.enter_context(nc.semaphore(f"sem_{e}")) for e in Emit.ENG}
    E = Emit()

    def dump(name, ap):
        if name not in dbg:
            return
        h = nc.dram_tensor("dbg_" + name, list(ap.shape), F32, kind="ExternalOutput").ap()
        dbg_h[name] = h
        for e in ("v", "a", "T"):
            E.wait("g", e)
        E.g(DMA(h, ap), inc=16)
        E.wait("y", "g")
        E.wait("v", "g")
        E.wait("a", "g")
        E.wait("t", "g")

    def fm(slab, j, t0=0, tn=T_O):
        return slab[:, j * T_O + t0: j * T_O + t0 + tn]

    def load_const():
        E.v(MEMSET(cst[:, 0:1], 1e-8), 1)
        E.v(MEMSET(cst[:, 1:2], 1.0), 1)
        E.v(MEMSET(cst[:, 2:3], 1e-5), 1)
        E.y(DMA(ones[:], din["ones128"][:]), 16)
        E.y(DMA(eye[:], din["eye128"][:]), 16)
        E.y(DMA(mUT[:], din["maskUT"][:]), 16)
        E.y(DMA(oK1[:], din["onesK1"][:]), 16)
        E.y(DMA(hA[:], din["h0c"][:]), 16)
        E.y(DMA(m2cS[:], din["m2c0"][:]), 16)
        E.y(DMA(hsh[:], din["hshift"][:]), 16)
        def ldvec(sb, name, nch, j=128):
            ld = E.y(DMAS(sb[0:j, 0:nch], din[name].rearrange("(m j) -> j m", j=j)), 16)
            E.wait("y", "y", ld)
        ldvec(bias_dt, "b_dt", 8); ldvec(bias_out, "b_out", 8)
        ldvec(bias_v, "v_b", 8); ldvec(bias_o, "o_b", 8)
        ldvec(bias_f1, "f1_b", 32); ldvec(bias_f2, "f2_b", 8)
        ldvec(w_ln1, "ln1_w", 8); ldvec(b_ln1, "ln1_b", 8)
        ldvec(w_ln2, "ln2_w", 8); ldvec(b_ln2, "ln2_b", 8)
        ld = E.y(DMAS(bias_qk[0:64, 0:1], din["q_b"].rearrange("(m j) -> j m", j=64)), 16)
        E.wait("y", "y", ld)
        ld = E.y(DMAS(bias_qk[64:128, 0:1], din["k_b"].rearrange("(m j) -> j m", j=64)), 16)
        E.wait("y", "y", ld)
        ld = E.y(DMAS(smA[0:1, 0:NST], din["A_log"].rearrange("(o n) -> o n", o=1)), 16)
        E.wait("y", "y", ld)
        E.wait("a", "y")
        E.a(ACTV(smB[0:1, 0:NST], smA[0:1, 0:NST], AF.Exp), 1)
        E.wait("v", "a")
        E.v(TS(smC[0:1, 0:NST], smB[0:1, 0:NST], -1.0, ALU.mult), 1)
        E.wait("t", "v")
        E.t(MM(PB[0][:, 0:NST], oK1[:], smC[0:1, 0:NST]), 1)
        E.wait("v", "T")
        E.v(CPY(aRep[:], PB[0][:, 0:NST]), 1)
        E.wait("y", "v")
        E.wait("y", "a")

    slab_use = {"wA": 0, "wB": 0, "wAb": 0, "wBb": 0}

    def stream_mm(src_h, K, Mtot, rhs_tiles, evac, n_free=T_O):
        nK = (K + 127) // 128
        kp = min(128, K)
        nM = Mtot // 128
        per_bank = 512 // n_free
        group = 8 * per_bank
        slabs = [("wA", wA), ("wB", wB)]
        for m0 in range(0, nM, group):
            gN = min(group, nM - m0)
            E.wait("t", "v")
            E.wait("t", "a")
            for k in range(nK):
                nm, sl = slabs[k % 2]
                E.wait("y", "t", slab_use[nm])
                ld = E.y(DMA(sl[0:kp, 0:gN * 128],
                             src_h[k * 128:k * 128 + kp, m0 * 128:(m0 + gN) * 128]), 16)
                E.wait("t", "y", ld)
                rhs = rhs_tiles(k)
                for mi in range(gN):
                    bank = PB[(mi * n_free) // 512]
                    off = (mi * n_free) % 512
                    E.t(MM(bank[:, off:off + n_free], sl[0:kp, mi * 128:(mi + 1) * 128],
                           rhs, start=(k == 0 and off == 0), stop=(k == nK - 1)),
                        inc=(1 if mi == gN - 1 else 0))
                slab_use[nm] = E.c["t"]
            for mi in range(gN):
                bank = PB[(mi * n_free) // 512]
                off = (mi * n_free) % 512
                evac(m0 + mi, bank[:, off:off + n_free])

    def layernorm_fm(src, dst, affine=None, eps=1e-5, add_into=None, sqbuf=None):
        sq = sc2 if sqbuf is None else sqbuf
        for j in range(8):
            E.wait("a", "v")
            E.a(ACTV(fm(sq, j), fm(src, j), AF.Square), 1)
        E.wait("t", "a")
        E.wait("t", "v")
        for j in range(8):
            E.t(MM(PB[6][:, 0:T_O], ones[:], fm(src, j), start=(j == 0), stop=(j == 7)),
                inc=(1 if j == 7 else 0))
        for j in range(8):
            E.t(MM(PB[7][:, 0:T_O], ones[:], fm(sq, j), start=(j == 0), stop=(j == 7)),
                inc=(1 if j == 7 else 0))
        E.wait("v", "T")
        E.v(TS(stA[:], PB[6][:, 0:T_O], 1.0 / D, ALU.mult), 1)
        E.v(TS(stB[:], PB[7][:, 0:T_O], 1.0 / D, ALU.mult), 1)
        E.v(TT(stC[:], stA[:], stA[:], ALU.mult), 1)
        E.v(TT(stB[:], stB[:], stC[:], ALU.subtract), 1)
        E.wait("a", "v")
        E.a(ACTV(stC[:], stB[:], AF.Ln, bias=cst[:, 2:3]), 1)
        E.a(ACTV(stD[:], stC[:], AF.Exp, scale=-0.5), 1)
        E.wait("v", "a")
        for j in range(8):
            E.v(TT(fm(sq, j), fm(src, j), stA[:], ALU.subtract), 1)
            E.v(TT(fm(sq, j), fm(sq, j), stD[:], ALU.mult), 1)
            if affine is not None:
                wt, bt = affine
                E.v(TS(fm(sq, j), fm(sq, j), wt[:, j:j + 1], ALU.mult, bt[:, j:j + 1], ALU.add), 1)
            if add_into is None:
                E.v(CPY(fm(dst, j), fm(sq, j)), 1)
            else:
                E.v(TT(fm(dst, j), fm(add_into, j), fm(sq, j), ALU.add), 1)

    def tile_transpose(src_ap, dst_ap):
        kp = src_ap.shape[0]
        bp = src_ap.base_partition()
        E.wait("t", "a")
        pe = E.t(TRN(PB[5][0:src_ap.shape[1], 0:src_ap.shape[0]], src_ap,
                     eye[bp:bp + kp, bp:bp + kp]), 1)
        E.wait("a", "T", pe)
        return E.a(SCOPY(dst_ap, PB[5][0:src_ap.shape[1], 0:src_ap.shape[0]]), 1)

    def prep_transpose(src_h, dst_h, Ksrc, Msrc):
        for r in range(Ksrc // 128):
            for cix in range(Msrc // 128):
                ld = E.y(DMA(f1a[:], src_h[r * 128:(r + 1) * 128, cix * 128:(cix + 1) * 128]), 16)
                E.wait("t", "y", ld)
                ae = tile_transpose(f1a[:], f1b[:])
                E.wait("g", "a", ae)
                gd = E.g(DMA(dst_h[cix * 128:(cix + 1) * 128, r * 128:(r + 1) * 128], f1b[:]), 16)
                E.wait("y", "g", gd)
                E.wait("t", "g", gd)

    def prep(upto=9):
        prep_transpose(din["v_w"], scr["vT"], D, D)
        if upto < 1:
            return
        prep_transpose(din["o_w"], scr["oT"], D, D)
        prep_transpose(din["f1_w"], scr["f1T"], DH, D)
        prep_transpose(din["f2_w"], scr["f2T"], D, DH)
        if upto < 2:
            return
        for name, coff in (("q_w", 0), ("k_w", 64)):
            for ki in range(8):
                ld = E.y(DMA(f1a[0:64, 0:128], din[name][:, ki * 128:(ki + 1) * 128]), 16)
                E.wait("t", "y", ld)
                ae = tile_transpose(f1a[0:64, 0:128], f1b[:, 0:64])
                E.wait("g", "a", ae)
                gd = E.g(DMA(scr["qkT"][ki * 128:(ki + 1) * 128, coff:coff + 64], f1b[:, 0:64]), 16)
                E.wait("y", "g", gd)
                E.wait("t", "g", gd)

        def quant_w(src_h, dst_h, sc_tile, Kd, Md):
            import os as _os
            _qw = int(_os.environ.get("QW", "99"))
            nsl = Md // 128
            nseg = (Kd + 2047) // 2048
            E.v(MEMSET(smA[:, 0:1], 0.0), 1)
            for i in range(nsl):
                for sgi in range(nseg):
                    k0 = sgi * 2048
                    kw = min(2048, Kd - k0)
                    nm, sl = ("wA", wA) if (i * nseg + sgi) % 2 == 0 else ("wB", wB)
                    E.wait("y", "v", E.c["v"])
                    ld = E.y(DMA(sl[:, 0:kw], src_h[i * 128:(i + 1) * 128, k0:k0 + kw]), 16)
                    E.wait("v", "y", ld)
                    E.v(REDA(smC[:, 0:1], sl[:, 0:kw].rearrange("p (a b) -> p a b", a=1), ALU.add), 1)
                    E.dr("v")
                    E.v(TT(smA[:, 0:1], smA[:, 0:1], smC[:, 0:1], ALU.add), 1)
            if _qw < 1:
                return
            E.wait("t", "v")
            E.t(MM(PB[5][:, 0:1], ones[:], smA[:, 0:1]), 1)
            E.wait("v", "T")
            E.v(TS(sc_tile[:], PB[5][:, 0:1], 1.0 / (Kd * Md), ALU.mult, 1e-5, ALU.max), 1)
            E.dr("v")
            E.v(TS(smC[:, 2:3], sc_tile[:], 0.5, ALU.mult), 1)
            E.v(TS(smC[:, 3:4], sc_tile[:], -0.5, ALU.mult), 1)
            E.dr("v")
            if _qw < 2:
                return
            nsl = min(nsl, int(_os.environ.get("QWN", "999")))
            for i in range(nsl):
                for sgi in range(nseg):
                    k0 = sgi * 2048
                    kw = min(2048, Kd - k0)
                    nm, sl = ("wA", wA) if (i * nseg + sgi) % 2 == 0 else ("wB", wB)
                    E.wait("y", "v", E.c["v"])
                    ld = E.y(DMA(sl[:, 0:kw], src_h[i * 128:(i + 1) * 128, k0:k0 + kw]), 16)
                    E.wait("v", "y", ld)
                    E.wait("v", "a")
                    _qs = int(_os.environ.get("QS", "9"))
                    if _qs >= 2:
                        E.v(TT(sc1[:, 0:kw], sl[:, 0:kw], smC[:, 2:3].broadcast_to((128, kw)), ALU.is_ge), 1)
                        E.v(TT(sc2[:, 0:kw], sl[:, 0:kw], smC[:, 3:4].broadcast_to((128, kw)), ALU.is_le), 1)
                        vv = E.v(TT(sc1[:, 0:kw], sc1[:, 0:kw], sc2[:, 0:kw], ALU.subtract), 1)
                        E.wait("t", "v", vv)
                    if _qs >= 3:
                        for kt in range(kw // 128):
                            E.wait("t", "a")
                            pe = E.t(TRN(PB[5][:, 0:128], sc1[:, kt * 128:(kt + 1) * 128], eye[:]), 1)
                            E.wait("a", "T", pe)
                            E.a(SCOPY(hbf[:, k0 + kt * 128:k0 + (kt + 1) * 128], PB[5][:, 0:128]), 1)
                if int(_os.environ.get("QS", "9")) >= 4:
                    E.wait("g", "a")
                    gd = E.g(DMA(dst_h[0:Kd, i * 128:(i + 1) * 128].rearrange("(kt p) m -> p kt m", p=128),
                                 hbf[:, 0:Kd].rearrange("p (kt m) -> p kt m", m=128)), 16)
                    E.wait("v", "g", gd)
                    E.wait("a", "g", gd)
        if upto < 3:
            return
        quant_w(din["w1"], scr["w1qT"], s1S, D, DH)
        if upto < 4:
            return
        quant_w(din["w2"], scr["w2qT"], s2Sq, DH, D)
        if upto < 5:
            return

        for fi in range(32):
            for ii in range(8):
                ld = E.y(DMA(f1a[:], din["W_exp"][ii * 128:(ii + 1) * 128, fi * 128:(fi + 1) * 128]), 16)
                E.wait("t", "y", ld)
                ae = tile_transpose(f1a[:], f1b[:])
                E.wait("g", "a", ae)
                gd = E.g(DMA(scr["curA"][fi, :, ii * 128:(ii + 1) * 128], f1b[:]), 16)
                E.wait("y", "g", gd)
                E.wait("t", "g", gd)
            ld = E.y(DMAS(f1b[:, 0:1], din["b_exp"].rearrange("(f j) -> j f", j=128)[:, fi:fi + 1]), 16)
            E.wait("y", "y", ld)
            E.wait("g", "y", ld)
            gd = E.g(DMAS(scr["curA"][fi, :, D:D + 1], f1b[:, 0:1]), 16)
            E.wait("g", "g", gd)
            E.wait("y", "g", gd)

        def compose(tiles_h, width, curs, reverse=False):
            order = list(range(11, -1, -1)) if reverse else list(range(12))
            src_i = 0
            for s in order:
                stride_blk = (2 ** s) // 128
                src, dst = curs[src_i % 2], curs[(src_i + 1) % 2]
                for m in range(32):
                    partners = [(0, m)] if stride_blk == 0 else [(0, m), (1, m ^ stride_blk)]
                    for nf0 in range(0, width, 512):
                        nf = min(512, width - nf0)
                        for si, kblk in partners:
                            E.wait("y", "t", E.c["t"])
                            ldw = E.y(DMA(f1a[:], tiles_h[s, m, si]), 16)
                            dmaop = DMA if nf > 1 else DMAS
                            ldr = E.y(dmaop(wA[:, 0:nf], src[kblk, :, nf0:nf0 + nf]), 16)
                            if nf == 1:
                                E.wait("y", "y", ldr)
                            E.wait("t", "y", ldr)
                            E.t(MM(PB[4][:, 0:nf], f1a[:], wA[:, 0:nf],
                                   start=(si == 0), stop=(si == len(partners) - 1)), inc=1)
                        E.wait("a", "T")
                        ae = E.a(SCOPY(wB[:, 0:nf], PB[4][:, 0:nf]), 1)
                        E.wait("g", "a", ae)
                        dmaop = DMA if nf > 1 else DMAS
                        gd = E.g(dmaop(dst[m, :, nf0:nf0 + nf], wB[:, 0:nf]), 16)
                        if nf == 1:
                            E.wait("g", "g", gd)
                        E.wait("y", "g", gd)
                src_i += 1
            return curs[src_i % 2]

        if upto < 6:
            return
        curF = compose(din["bfU"], D + 1, (scr["curA"], scr["curB"]))
        if upto < 7:
            return
        for fi in range(32):
            for ii in range(8):
                ld = E.y(DMA(f1a[:], curF[fi, :, ii * 128:(ii + 1) * 128]), 16)
                E.wait("t", "y", ld)
                ae = tile_transpose(f1a[:], f1b[:])
                E.wait("g", "a", ae)
                gd = E.g(DMA(scr["w1c"][ii * 128:(ii + 1) * 128, fi * 128:(fi + 1) * 128], f1b[:]), 16)
                E.wait("y", "g", gd)
                E.wait("t", "g", gd)
            ld = E.y(DMAS(f1b[:, 0:1], curF[fi, :, D:D + 1]), 16)
            E.wait("y", "y", ld)
            E.wait("g", "y", ld)
            gd = E.g(DMAS(scr["b1c"].rearrange("(f j) -> j f", j=128)[:, fi:fi + 1], f1b[:, 0:1]), 16)
            E.wait("g", "g", gd)
            E.wait("y", "g", gd)
        E.wait("y", "g")
        ld = E.y(DMAS(sc1[:, 0:32], scr["b1c"].rearrange("(m j) -> j m", j=128)), 16)
        E.wait("y", "y", ld)
        ld2 = E.y(DMAS(sc2[:, 0:32], din["up_bias"].rearrange("(m j) -> j m", j=128)), 16)
        E.wait("y", "y", ld2)
        E.wait("v", "y", ld2)
        E.v(TT(bias_b1[:, 0:32], sc1[:, 0:32], sc2[:, 0:32], ALU.add), 1)
        ld = E.y(DMAS(sc1[:, 0:8], din["b_con"].rearrange("(m j) -> j m", j=128)), 16)
        E.wait("y", "y", ld)
        ld2 = E.y(DMAS(sc2[:, 0:8], din["down_bias"].rearrange("(m j) -> j m", j=128)), 16)
        E.wait("y", "y", ld2)
        E.wait("v", "y", ld2)
        E.v(TT(bias_con[:, 0:8], sc1[:, 0:8], sc2[:, 0:8], ALU.add), 1)
        for fi in range(32):
            E.wait("y", "g", E.c["g"])
            ld = E.y(DMA(wA[:, 0:D], din["W_con"][fi * 128:(fi + 1) * 128, :]), 16)
            E.wait("g", "y", ld)
            E.g(DMA(scr["cu2A"][fi], wA[:, 0:D]), 16)
        if upto < 8:
            return
        E.wait("y", "g")
        cu2F = compose(din["bfD"], D, (scr["cu2A"], scr["cu2B"]), reverse=True)
        for fi in range(32):
            E.wait("y", "g", E.c["g"])
            ld = E.y(DMA(wA[:, 0:D], cu2F[fi]), 16)
            E.wait("g", "y", ld)
            E.g(DMA(scr["w2c"][fi * 128:(fi + 1) * 128, :], wA[:, 0:D]), 16)
        E.wait("y", "g")

    def load_x_chunk(c):
        E.wait("y", "g")
        for tt in range(T_O // 128):
            t0 = c * T_O + tt * 128
            ld = E.y(DMA(sc1[:, tt * D:(tt + 1) * D], x_hbm[t0:t0 + 128, :]), 16)
            E.wait("t", "y", ld)
            for j in range(8):
                tile_transpose(sc1[:, tt * D + j * 128: tt * D + (j + 1) * 128],
                               xR[:, j * T_O + tt * 128: j * T_O + (tt + 1) * 128])
        E.wait("v", "a")

    def store_x_chunk(c):
        E.wait("t", "v")
        for tt in range(T_O // 128):
            t0 = c * T_O + tt * 128
            for j in range(8):
                tile_transpose(xR[:, j * T_O + tt * 128: j * T_O + (tt + 1) * 128],
                               sc1[:, tt * D + j * 128: tt * D + (j + 1) * 128])
            E.wait("g", "a")
            E.g(DMA(out_hbm[t0:t0 + 128, :], sc1[:, tt * D:(tt + 1) * D]), 16)

    def l0_pre(c):
        layernorm_fm(xR, xN)
        dump(f"x1_{c}", xN[:])
        def evac_inz(m, ps):
            dst, j = (xI, m) if m < 8 else (zS, m - 8)
            E.wait("a", "T")
            E.a(ACTV(fm(dst, j), ps, AF.Silu), 1)
        stream_mm(din["W_in"], D, 2 * D, lambda k: fm(xN, k), evac_inz)
        dump(f"xin_{c}", xI[:])
        dump(f"zs_{c}", zS[:])
        def evac_xp(m, ps):
            E.wait("v", "T")
            E.v(CPY(xpS[:], ps), 1)
        stream_mm(din["W_xproj"], D, 128, lambda k: fm(xI, k), evac_xp)
        def evac_dt(m, ps):
            E.wait("v", "T")
            E.v(TS(sc1[:, 0:T_O], ps, bias_dt[:, m:m + 1], ALU.add), 1)
            E.wait("a", "v")
            E.a(ACTV(sc1[:, T_O:2 * T_O], sc1[:, 0:T_O], AF.Abs), 1)
            E.a(ACTV(sc1[:, 2 * T_O:3 * T_O], sc1[:, T_O:2 * T_O], AF.Exp, scale=-1.0), 1)
            E.a(ACTV(sc1[:, T_O:2 * T_O], sc1[:, 2 * T_O:3 * T_O], AF.Ln, bias=cst[:, 1:2]), 1)
            E.wait("v", "a")
            E.v(TS(sc1[:, 0:T_O], sc1[:, 0:T_O], 0.0, ALU.max), 1)
            E.v(TT(dtS[:, m * T_O:(m + 1) * T_O], sc1[:, 0:T_O], sc1[:, T_O:2 * T_O], ALU.add), 1)
        stream_mm(din["W_dt"], 64, D, lambda k: xpS[0:64, :], evac_dt)
        dump(f"dt_{c}", dtS[:])
        for j in range(8):
            E.v(TT(dxS[:, j * T_O:(j + 1) * T_O], dtS[:, j * T_O:(j + 1) * T_O], fm(xI, j), ALU.mult), 1)
        E.wait("t", "v")
        for k in range(8):
            E.wait("y", "t", slab_use["wA"])
            ld = E.y(DMA(wA[:, 0:32], din["W_res"][k * 128:(k + 1) * 128, :]), 16)
            E.wait("t", "y", ld)
            E.t(MM(PB[0][0:32, 0:T_O], wA[:, 0:32], fm(xI, k), start=(k == 0), stop=(k == 7)), inc=1)
            slab_use["wA"] = E.c["t"]
        E.wait("v", "T")
        E.v(CPY(xrS[:], PB[0][0:32, 0:T_O]), 1)
        dump(f"xres_{c}", xrS[:])

    def scan_stage_inner(c, i0):
        dt_v = dtS[:].rearrange("p (d t) -> p d t", d=8)[:, :, i0:i0 + T_I]
        dt_b = dt_v.transpose((0, 2, 1)).unsqueeze(3).broadcast_to((128, T_I, 8, NST))
        a_b = aRep[:].unsqueeze(1).unsqueeze(1).broadcast_to((128, T_I, 8, NST))
        g3 = gS[:].rearrange("p (t d n) -> p t d n", t=T_I, d=8)
        E.wait("v", "a")
        E.v(TT(g3, dt_b, a_b, ALU.mult), 1)
        E.wait("a", "v")
        E.a(ACTV(gS[:], gS[:], AF.Exp), 1)
        E.wait("v", "a")
        E.v(TS(gS[:], gS[:], -KAPPA, ALU.mult, KAPPA, ALU.add), 1)
        E.wait("t", "v")
        pe = E.t(TRN(PB[5][0:T_I, 0:64], xpS[64:128, i0:i0 + T_I], eye[64:128, 64:128]), 1)
        E.wait("v", "T", pe)
        E.v(CPY(flT[:, 0:64], PB[5][0:T_I, 0:64]), 1)
        E.wait("t", "v")
        pe = E.t(TRN(PB[5][0:T_I, 64:96], xrS[:, i0:i0 + T_I], eye[0:32, 0:32]), 1)
        E.wait("v", "T", pe)
        vv = E.v(CPY(flT[:, 64:96], PB[5][0:T_I, 64:96]), 1)
        E.wait("g", "v", vv)
        gd = E.g(DMA(flF[:], flT[:]), 16)
        E.wait("t", "g", gd)
        for seg in range(0, T_I * 96, 512):
            sn = min(512, T_I * 96 - seg)
            pe = E.t(MM(PB[4][:, 0:sn], oK1[:], flF[:, seg:seg + sn]), 1)
            E.wait("v", "T", pe)
            E.v(CPY(bcR[:, seg:seg + sn], PB[4][:, 0:sn]), 1)
            E.wait("t", "v")
        bc3 = bcR[:].rearrange("p (t f) -> p t f", f=96)
        xr0 = bc3[:, :, 64:96].rearrange("p t (n c) -> p t n c", c=2)[:, :, :, 0:1]
        E.v(TS(xr0, xr0, 1e-10, ALU.add), 1)
        xr_q = bc3[:, :, 64:96]
        sq = sc1[:, 0:T_I * 32].rearrange("p (t q) -> p t q", q=32)
        E.v(TT(sq, xr_q, xr_q, ALU.mult), 1)
        v4 = sc1[:, 0:T_I * 32].rearrange("p (t n c) -> p t n c", n=NST, c=2)
        v_c0 = v4[:, :, :, 0:1].broadcast_to((128, T_I, NST, 2))
        v_c1 = v4[:, :, :, 1:2].broadcast_to((128, T_I, NST, 2))
        E.v(TT(rxS[:].rearrange("p (t n c) -> p t n c", n=NST, c=2), v_c0, v_c1, ALU.add), 1)
        E.wait("a", "v")
        E.a(ACTV(sc1[:, T_I * 32:2 * T_I * 32], rxS[:], AF.Ln), 1)
        E.a(ACTV(sc1[:, T_I * 32:2 * T_I * 32], sc1[:, T_I * 32:2 * T_I * 32], AF.Exp, scale=-0.5), 1)
        E.wait("v", "a")
        E.v(TS(rxS[:], sc1[:, T_I * 32:2 * T_I * 32], -0.5, ALU.mult), 1)
        if c == 0 and i0 == 0:
            dump("sd_rx2", rxS[:])
        dx_v = dxS[:].rearrange("p (d t) -> p d t", d=8)[:, :, i0:i0 + T_I]
        dx_b = dx_v.transpose((0, 2, 1)).unsqueeze(3).broadcast_to((128, T_I, 8, 2 * NST))
        b_b = bc3[:, :, 0:32].unsqueeze(2).broadcast_to((128, T_I, 8, 2 * NST))
        bx4 = bxS[:].rearrange("p (t d q) -> p t d q", t=T_I, d=8)
        E.v(TT(bx4, dx_b, b_b, ALU.mult), 1)

    def scan_step(c, i0, ti, h_cur, h_nxt, first):
        bc3 = bcR[:].rearrange("p (t f) -> p t f", f=96)
        C_t = bc3[:, ti, 32:64].rearrange("p (n c) -> p n c", c=2)
        xr_t = bc3[:, ti, 64:96]
        rx_t = rxS[:, ti * 32:(ti + 1) * 32]
        g_t = gS[:, ti * 128:(ti + 1) * 128]
        bx_t = bxS[:, ti * 256:(ti + 1) * 256]
        t_loc = i0 + ti
        scr256 = sc1[:, 3 * T_O:3 * T_O + 256]
        h_v = h_cur[:].rearrange("p (d n c) -> p n c d", d=8, n=NST)
        vv = E.v(RED(smA[:, 0:32], h_v), 1)
        E.wait("t", "v", vv)
        pe = E.t(MM(PB[3][:, 0:32], ones[:], smA[:, 0:32]), 1)
        E.wait("v", "T", pe)
        def nslots(ap32, ci):
            w4 = ap32.rearrange("p (n c) -> p n c", c=2)
            return w4[:, :, ci:ci + 1].squeeze(2).unsqueeze(1).broadcast_to((128, 8, NST))
        def cdup(ap32):
            w4 = ap32.rearrange("p (n c) -> p n c", c=2)
            return (w4[:, :, 0:1].broadcast_to((128, NST, 2)),
                    w4[:, :, 1:2].broadcast_to((128, NST, 2)))
        E.v(TT(smB[:, 0:32], PB[3][:, 0:32], hsh[:, 0:32], ALU.add), 1)
        E.dr("v")
        E.v(TT(smC[:, 0:32], smB[:, 0:32], xr_t, ALU.mult), 1)
        E.v(TT(smC[:, 32:64], smB[:, 0:32], smB[:, 0:32], ALU.mult), 1)
        E.dr("v")
        nm0, nm1 = cdup(smC[:, 0:32])
        E.v(TT(smA[:, 0:32].rearrange("p (n c) -> p n c", c=2), nm0, nm1, ALU.add), 1)
        sq0, sq1 = cdup(smC[:, 32:64])
        E.v(TT(smA[:, 32:64].rearrange("p (n c) -> p n c", c=2), sq0, sq1, ALU.add), 1)
        if not first:
            E.v(TT(f1a[:], s2S[:], s2S[:], ALU.mult), 1)
            E.v(TT(m2cS[:], f1a[:], m2qS[:], ALU.mult), 1)
        E.wait("a", "v")
        E.a(ACTV(smA[:, 32:64], smA[:, 32:64], AF.Ln), 1)
        E.a(ACTV(smA[:, 32:64], smA[:, 32:64], AF.Exp, scale=-0.5), 1)
        E.a(ACTV(f1b[:], m2cS[:], AF.Ln, bias=cst[:, 0:1]), 1)
        E.a(ACTV(f1c[:], f1b[:], AF.Exp, scale=0.5), 1)
        aa = E.a(ACTV(f1b[:], f1c[:], AF.Ln, bias=cst[:, 0:1]), 1)
        E.wait("v", "a", aa)
        f1f3 = f1f[:].rearrange("p (d n) -> p d n", d=8)
        E.v(TT(f1f3, nslots(smA[:, 0:32], 0), nslots(smA[:, 32:64], 0), ALU.mult), 1)
        E.v(TT(f1f3, f1f3, nslots(rx_t, 0), ALU.mult), 1)
        E.v(TT(f1d[:], g_t, f1b[:], ALU.mult), 1)
        E.v(TT(f1a[:], f1d[:], f1f[:], ALU.mult), 1)
        uv = E.v(STT(f1a[:], f1d[:], 0.5, f1a[:], ALU.mult, ALU.add), 1)
        if first:
            dump("sd_u", f1a[:])
        E.wait("a", "v", uv)
        ea = E.a(ACTV(f1e[:], f1a[:], AF.Exp), 1)
        E.wait("v", "a", ea)
        E.v(RECIP(f1f[:], f1c[:]), 1)
        E.v(TS(f1f[:], f1f[:], 1e-8, ALU.mult, 1.0, ALU.add), 1)
        E.v(TT(f1e[:], f1e[:], f1f[:], ALU.mult), 1)

        s1b = f1e[:].rearrange("p (d n) -> p d n", d=8).unsqueeze(3).broadcast_to((128, 8, NST, 2))
        h4o = h_nxt[:].rearrange("p (d n c) -> p d n c", d=8, n=NST)
        h4i = h_cur[:].rearrange("p (d n c) -> p d n c", d=8, n=NST)
        E.v(TT(h4o, h4i, s1b, ALU.mult), 1)
        E.v(TT(h_nxt[:], h_nxt[:], bx_t, ALU.add), 1)
        if first:
            dump("sd_hpre", h_nxt[:]); dump("sd_s1", f1e[:])
        E.v(TT(scr256, h_nxt[:], h_nxt[:], ALU.mult), 1)
        m2v = E.v(RED(m2qS[:], scr256.rearrange("p (d n c) -> p d n c", d=8, n=NST)), 1)
        E.wait("a", "v", m2v)
        E.a(ACTV(f1b[:], m2qS[:], AF.Ln, bias=cst[:, 0:1]), 1)
        E.a(ACTV(f1c[:], f1b[:], AF.Exp, scale=0.5), 1)
        ta = E.a(ACTV(f1d[:], f1c[:], AF.Exp, scale=-2.0), 1)
        E.wait("v", "a", ta)
        E.v(TS(f1a[:], f1d[:], 1.0, ALU.add), 1)
        E.v(RECIP(f1a[:], f1a[:]), 1)
        E.v(TS(f1d[:], f1d[:], -1.0, ALU.mult, 1.0, ALU.add), 1)
        E.v(TT(f1d[:], f1d[:], f1a[:], ALU.mult), 1)
        E.v(RECIP(f1c[:], f1c[:]), 1)
        E.v(TT(s2S[:], f1d[:], f1c[:], ALU.mult), 1)
        if first:
            dump("sd_s2v", s2S[:]); dump("sd_m2qv", m2qS[:])
        s2b = s2S[:].rearrange("p (d n) -> p d n", d=8).unsqueeze(3).broadcast_to((128, 8, NST, 2))
        E.v(TT(h4o, h4o, s2b, ALU.mult), 1)
        cb = C_t.unsqueeze(1).broadcast_to((128, 8, NST, 2))
        E.v(TT(scr256.rearrange("p (d n c) -> p d n c", d=8, n=NST), h4o, cb, ALU.mult), 1)
        yv = yS[:].rearrange("p (d t) -> p d t", d=8)[:, :, t_loc:t_loc + 1]
        E.v(RED(yv, scr256.rearrange("p (d q) -> p d q", d=8)), 1)

    def scan_chunk(c, h_pair):
        for i0 in range(0, T_O, T_I):
            scan_stage_inner(c, i0)
            if c == 0 and i0 == 0:
                dump("sd_gS", gS[:]); dump("sd_bcR", bcR[:]); dump("sd_rxS", rxS[:])
                dump("sd_bxS", bxS[:])
            for ti in range(T_I):
                first = (c == 0 and i0 == 0 and ti == 0)
                scan_step(c, i0, ti, h_pair[0], h_pair[1], first)
                if c == 0 and i0 == 0 and ti == 0:
                    dump("sd_h1", h_pair[1][:]); dump("sd_smA", smA[:]); dump("sd_smB", smB[:])
                    dump("sd_s2S", s2S[:]); dump("sd_m2q", m2qS[:]); dump("sd_f1e", f1e[:])
                h_pair[0], h_pair[1] = h_pair[1], h_pair[0]
        E.dr("v")
        dump(f"y_{c}", yS[:])

    def l0_post(c):
        for j in range(8):
            E.v(TT(fm(zS, j), yS[:, j * T_O:(j + 1) * T_O], fm(zS, j), ALU.mult), 1)
        def evac_out(m, ps):
            E.wait("a", "T")
            E.a(ACTV(fm(xN, m), ps, AF.Identity, bias=bias_out[:, m:m + 1]), 1)
        stream_mm(din["W_out"], D, D, lambda k: fm(zS, k), evac_out)
        dump(f"aout_{c}", xN[:])
        layernorm_fm(xN, xR, add_into=xR)
        dump(f"xmid_{c}", xR[:])

    def big_ffn(c, w_up_h, w_dn_h, gelu_bias, out_bias, dst, add_src, dbg_gelu=False):
        E.wait("t", "v")
        E.wait("t", "a")
        for g8 in range(4):
            for ki in range(8):
                nm, sl = ("wA", wA) if ki % 2 == 0 else ("wB", wB)
                E.wait("y", "t", slab_use[nm])
                ld = E.y(DMA(sl[:, 0:1024], w_up_h[ki * 128:(ki + 1) * 128, g8 * 1024:(g8 + 1) * 1024]), 16)
                E.wait("t", "y", ld)
                if ki == 0:
                    E.wait("t", "a")
                for mi in range(8):
                    bank = PB[4 + (mi * T_O) // 512]
                    off = (mi * T_O) % 512
                    E.t(MM(bank[:, off:off + T_O], sl[:, mi * 128:(mi + 1) * 128], fm(xN, ki),
                           start=(ki == 0 and off == 0), stop=(ki == 7)), inc=(1 if mi == 7 else 0))
                slab_use[nm] = E.c["t"]
            E.wait("a", "T")
            for mi in range(8):
                ko = g8 * 8 + mi
                bank = PB[4 + (mi * T_O) // 512]
                off = (mi * T_O) % 512
                E.a(ACTV(sc1[:, mi * T_O:(mi + 1) * T_O], bank[:, off:off + T_O], AF.Gelu,
                         bias=gelu_bias[:, ko:ko + 1]), 1)
            if dbg_gelu and g8 == 0:
                dump("sd_gel0", sc1[:])
            E.wait("t", "a")
            if g8 == 0:
                E.wait("t", "v")
            for mi in range(8):
                ko = g8 * 8 + mi
                nm, sl = ("wA", wA) if mi % 2 == 0 else ("wB", wB)
                E.wait("y", "t", slab_use[nm])
                ld = E.y(DMA(sl[:, 0:D], w_dn_h[ko * 128:(ko + 1) * 128, :]), 16)
                E.wait("t", "y", ld)
                for m in range(8):
                    bank = PB[(m * T_O) // 512]
                    off = (m * T_O) % 512
                    E.t(MM(bank[:, off:off + T_O], sl[:, m * 128:(m + 1) * 128],
                           sc1[:, mi * T_O:(mi + 1) * T_O],
                           start=(ko == 0 and off == 0), stop=(ko == 31)), inc=(1 if m == 7 else 0))
                slab_use[nm] = E.c["t"]
        E.wait("v", "T")
        for m in range(8):
            bank = PB[(m * T_O) // 512]
            off = (m * T_O) % 512
            E.v(TS(sc2[:, m * T_O:(m + 1) * T_O], bank[:, off:off + T_O],
                   out_bias[:, m:m + 1], ALU.add), 1)
            if add_src is None:
                E.v(CPY(fm(dst, m), sc2[:, m * T_O:(m + 1) * T_O]), 1)
            else:
                E.v(TT(fm(dst, m), fm(add_src, m), sc2[:, m * T_O:(m + 1) * T_O], ALU.add), 1)

    def l0_ffn(c):
        layernorm_fm(xR, xN)
        big_ffn(c, scr["w1c"], scr["w2c"], bias_b1, bias_con, xN, None)
        dump(f"fout_{c}", xN[:])
        layernorm_fm(xN, xR, add_into=xR)
        dump(f"x2_{c}", xR[:])

    def l1_attn(c):
        layernorm_fm(xR, xT3)
        layernorm_fm(xT3, xN, affine=(w_ln1, b_ln1))
        def evac_qk(m, ps):
            E.wait("v", "T")
            E.v(TS(sc1[:, 0:T_O], ps, bias_qk[:, 0:1], ALU.add), 1)
            E.v(TS(sc1[:, T_O:2 * T_O], sc1[:, 0:T_O], 0.0, ALU.min), 1)
            E.wait("a", "v")
            E.a(ACTV(sc1[:, T_O:2 * T_O], sc1[:, T_O:2 * T_O], AF.Exp), 1)
            E.wait("v", "a")
            E.v(TS(sc1[:, 0:T_O], sc1[:, 0:T_O], 0.0, ALU.max), 1)
            E.v(TT(qkS[:], sc1[:, 0:T_O], sc1[:, T_O:2 * T_O], ALU.add), 1)
        stream_mm(scr["qkT"], D, 128, lambda k: fm(xN, k), evac_qk)
        dump(f"qk_{c}", qkS[:])
        dump(f"xn1_{c}", xN[:])
        E.wait("g", "v")
        gd = E.g(DMA(kSS[:, :], qkS[64:128, :]), 16)
        E.wait("t", "g", gd)
        E.wait("v", "g", gd)
        def evac_v(m, ps):
            E.wait("a", "T")
            E.a(ACTV(fm(zS, m), ps, AF.Identity, bias=bias_v[:, m:m + 1]), 1)
        stream_mm(scr["vT"], D, D, lambda k: fm(xN, k), evac_v)
        E.wait("t", "a")
        for tt in range(T_O // 128):
            for j in range(8):
                tile_transpose(zS[:, j * T_O + tt * 128: j * T_O + (tt + 1) * 128],
                               vTM[:, tt * D + j * 128: tt * D + (j + 1) * 128])
        if c == 0:
            E.v(MEMSET(kvS[:, :], 0.0), 1)
        E.wait("t", "a")
        E.wait("t", "v")
        for tt in range(T_O // 128):
            q_t = qkS[0:64, tt * 128:(tt + 1) * 128]
            k_t = kSS[:, tt * 128:(tt + 1) * 128]
            E.wait("t", "v")
            pe = E.t(MM(PB[4][:, 0:128], k_t, q_t), 1)
            E.wait("v", "T", pe)
            vv = E.v(TT(atm[:], PB[4][:, 0:128], mUT[:], ALU.mult), 1)
            E.wait("t", "v", vv)
            E.t(MM(PB[4][0:1, 128:256], ones[:, 0:1], atm[:], start=True, stop=False), 0)
            dn = E.t(MM(PB[4][0:1, 128:256], kvS[:, D:D + 1], q_t, start=False, stop=True), 1)
            for j in range(8):
                bank = PB[j // 4]
                off = (j % 4) * 128
                E.t(MM(bank[:, off:off + 128], vTM[:, tt * D + j * 128: tt * D + (j + 1) * 128],
                       atm[:], start=(off == 0), stop=False), 0)
                E.t(MM(bank[:, off:off + 128], kvS[:, j * 128:(j + 1) * 128], q_t,
                       start=False, stop=True), 1)
            numT_done = E.c["t"]
            E.wait("v", "T", dn)
            E.v(TS(smA[0:1, 0:64], PB[4][0:1, 128:192], ATTN_EPS, ALU.add), 1)
            E.v(TS(smB[0:1, 0:64], PB[4][0:1, 192:256], ATTN_EPS, ALU.add), 1)
            E.dr("v")
            E.v(RECIP(smA[0:1, 0:64], smA[0:1, 0:64]), 1)
            vv = E.v(RECIP(smB[0:1, 0:64], smB[0:1, 0:64]), 1)
            E.wait("t", "v", vv)
            E.t(MM(PB[4][:, 256:320], oK1[:, 0:128], smA[0:1, 0:64], start=True, stop=True), 0)
            pe = E.t(MM(PB[4][:, 320:384], oK1[:, 0:128], smB[0:1, 0:64], start=True, stop=True), 1)
            E.wait("v", "T", pe)
            E.v(CPY(den_r[:], PB[4][:, 256:384]), 1)
            E.wait("v", "T", numT_done)
            for j in range(8):
                bank = PB[j // 4]
                off = (j % 4) * 128
                E.v(TT(sc2[:, j * T_O + tt * 128: j * T_O + (tt + 1) * 128],
                       bank[:, off:off + 128], den_r[:], ALU.mult), 1)
            pe = E.t(TRN(PB[4][0:128, 384:448], k_t, eye[0:64, 0:64]), 1)
            E.wait("v", "T", pe)
            vv = E.v(CPY(smC[:, 0:64], PB[4][0:128, 384:448]), 1)
            E.wait("t", "v", vv)
            E.t(MM(PB[6][0:64, 0:512], smC[:, 0:64], vTM[:, tt * D: tt * D + 512]), 0)
            E.t(MM(PB[7][0:64, 0:512], smC[:, 0:64], vTM[:, tt * D + 512: tt * D + 1024]), 0)
            pe = E.t(MM(PB[4][0:64, 448:449], smC[:, 0:64], ones[:, 0:1]), 1)
            E.wait("v", "T", pe)
            E.v(TT(kvS[:, 0:512], kvS[:, 0:512], PB[6][0:64, 0:512], ALU.add), 1)
            E.v(TT(kvS[:, 512:1024], kvS[:, 512:1024], PB[7][0:64, 0:512], ALU.add), 1)
            vv = E.v(TT(kvS[:, D:D + 1], kvS[:, D:D + 1], PB[4][0:64, 448:449], ALU.add), 1)
            E.wait("t", "v", vv)
        dump(f"attn_{c}", sc2[:])
        def evac_o(m, ps):
            E.wait("v", "T")
            E.v(TS(sc1[:, 0:T_O], ps, bias_o[:, m:m + 1], ALU.add), 1)
            E.v(TT(fm(xXX, m), fm(xT3, m), sc1[:, 0:T_O], ALU.add), 1)
        stream_mm(scr["oT"], D, D, lambda k: fm(sc2, k), evac_o)
        dump(f"xx_{c}", xXX[:])
        layernorm_fm(xXX, xN, affine=(w_ln2, b_ln2))
        dump(f"xn2_{c}", xN[:])
        big_ffn(c, scr["f1T"], scr["f2T"], bias_f1, bias_f2, xN, xXX, dbg_gelu=(c == 0))
        dump(f"tout_{c}", xN[:])
        layernorm_fm(xN, xR, add_into=xR)
        dump(f"x3r_{c}", xR[:])

    def round_ladder(ap, tmp, bits):
        E.v(TS(tmp, ap, -1.0, ALU.mult, -128.5, ALU.add), 1)
        for k in range(7, -1, -1):
            E.v(TS(bits, tmp, -float(2 ** k), ALU.is_le), 1)
            E.v(STT(tmp, bits, float(2 ** k), tmp, ALU.mult, ALU.add), 1)
        E.v(TT(ap, ap, tmp, ALU.add), 1)
        E.v(TS(ap, ap, 0.5, ALU.add), 1)

    def l1_ffn(c):
        layernorm_fm(xR, xN)
        dump(f"x4_{c}", xN[:])
        ntt = T_O // 128
        for tt in range(ntt):
            E.wait("t", "v")
            for j in range(8):
                tile_transpose(xN[:, j * T_O + tt * 128: j * T_O + (tt + 1) * 128],
                               xTM[:, j * 128:(j + 1) * 128])
            E.wait("v", "a")
            E.v(REDA(scq[:, tt:tt + 1], xTM[:].rearrange("p (a b) -> p a b", a=1), ALU.max), 1)
            E.dr("v")
            E.v(TS(scq[:, tt:tt + 1], scq[:, tt:tt + 1], 1e-5, ALU.max), 1)
            E.dr("v")
            E.v(RECIP(scq[:, tt:tt + 1], scq[:, tt:tt + 1]), 1)
            E.dr("v")
            E.v(TS(scq[:, tt:tt + 1], scq[:, tt:tt + 1], 127.0, ALU.mult), 1)
            E.dr("v")
            E.v(TS(sc1[:, 0:D], xTM[:], scq[:, tt:tt + 1], ALU.mult), 1)
            round_ladder(sc1[:, 0:D], sc1[:, D:2 * D], sc2[:, 0:D])
            if c == 0 and tt == 0:
                dump("sd_xqtm", sc1[:, 0:D]); dump("sd_scq", scq[:, 0:2])
            for j in range(8):
                E.wait("t", "v")
                pe = E.t(TRN(PB[5][:, 0:128], sc1[:, j * 128:(j + 1) * 128], eye[:]), 1)
                E.wait("v", "T", pe)
                E.v(CPY(qbf[:, j * T_O + tt * 128: j * T_O + (tt + 1) * 128], PB[5][:, 0:128]), 1)
        l1_ffn_mm(c, ntt)

    def l1_ffn_mm(c, ntt):
        if c == 0 and "sd_w2q" in dbg:
            ld = E.y(DMA(wBb[:, 0:D], scr["w2qT"][0:128, :]), 16)
            E.wait("v", "y", ld)
            E.v(CPY(sc1[:, 0:D], wBb[:, 0:D]), 1)
            dump("sd_w2q", sc1[:, 0:D])
            ld = E.y(DMA(wBb[:, 0:D], scr["w2qT"][2048:2176, :]), 16)
            E.wait("v", "y", ld)
            E.v(CPY(sc1[:, 0:D], wBb[:, 0:D]), 1)
            dump("sd_w2qB", sc1[:, 0:D])
            ld = E.y(DMA(wAb[:, 0:DH], scr["w1qT"][0:128, :]), 16)
            E.wait("v", "y", ld)
            E.v(CPY(sc1[:, 0:2048], wAb[:, 0:2048]), 1)
            dump("sd_w1q", sc1[:, 0:2048])
        for tt in range(ntt):
            vv = E.v(CPY(smB[:, 1:2], scq[:, tt:tt + 1]), 1)
            E.wait("t", "v", vv)
            pe = E.t(TRN(PB[4][0:1, 0:128], smB[:, 1:2], eye[:]), 1)
            E.wait("v", "T", pe)
            E.v(CPY(srow[0:1, tt * 128:(tt + 1) * 128], PB[4][0:1, 0:128]), 1)
        vv = E.c["v"]
        E.wait("t", "v", vv)
        pe = E.t(MM(PB[4][:, 128:128 + T_O], oK1[:], srow[:]), 1)
        E.wait("v", "T", pe)
        E.v(RECIP(scRep[:], PB[4][:, 128:128 + T_O]), 1)
        E.v(TS(scRep[:], scRep[:], s1S[:, 0:1], ALU.mult), 1)
        for tt in range(ntt):
            E.wait("t", "v")
            for k in range(8):
                nmb, slb = ("wAb", wAb) if k % 2 == 0 else ("wBb", wBb)
                E.wait("y", "t", slab_use[nmb])
                ld = E.y(DMA(slb[:, 0:DH], scr["w1qT"][k * 128:(k + 1) * 128, :]), 16)
                E.wait("t", "y", ld)
                if k == 0:
                    E.wait("t", "a")
                for m in range(32):
                    bank = PB[m // 4]
                    off = (m % 4) * 128
                    E.t(MM(bank[:, off:off + 128], slb[:, m * 128:(m + 1) * 128],
                           qbf[:, k * T_O + tt * 128:k * T_O + (tt + 1) * 128],
                           start=(k == 0 and off == 0), stop=(k == 7)), inc=(1 if m == 31 else 0))
                slab_use[nmb] = E.c["t"]
            E.wait("v", "T")
            for m in range(32):
                bank = PB[m // 4]
                off = (m % 4) * 128
                E.v(TT(hfm(m), bank[:, off:off + 128],
                       scRep[:, tt * 128:(tt + 1) * 128], ALU.mult), 1)
            E.wait("a", "v")
            for m in range(32):
                E.a(ACTV(hfm(m), hfm(m), AF.Gelu), 1)
            if c == 0 and tt == 0:
                dump("sd_hg1", dtS[:]); dump("sd_hg2", dxS[:]); dump("sd_screp", scRep[:])
            E.wait("t", "a")
            for m in range(32):
                dsth = xI if m < 16 else zS
                mm_ = m % 16
                tile_transpose(hfm(m), dsth[:, mm_ * 128:(mm_ + 1) * 128])
            E.wait("v", "a")
            E.v(REDA(smA[:, 0:1], xI[:, 0:2048].rearrange("p (a b) -> p a b", a=1), ALU.max), 1)
            E.v(REDA(smA[:, 1:2], zS[:, 0:2048].rearrange("p (a b) -> p a b", a=1), ALU.max), 1)
            E.dr("v")
            E.v(TT(smA[:, 0:1], smA[:, 0:1], smA[:, 1:2], ALU.max), 1)
            E.dr("v")
            E.v(TS(scqh[:], smA[:, 0:1], 1e-5, ALU.max), 1)
            E.dr("v")
            E.v(RECIP(scqh[:], scqh[:]), 1)
            E.dr("v")
            E.v(TS(scqh[:], scqh[:], 127.0, ALU.mult), 1)
            E.dr("v")
            for half, buf in ((0, xI), (1, zS)):
                E.v(TS(sc1[:, 0:2048], buf[:, 0:2048], scqh[:, 0:1], ALU.mult), 1)
                round_ladder(sc1[:, 0:2048], xN[:, 0:2048], buf[:, 0:2048])
                E.v(CPY(buf[:, 0:2048], sc1[:, 0:2048]), 1)
            if c == 0 and tt == 0:
                dump("sd_hq1", xI[:, 0:2048]); dump("sd_scqh", scqh[:])
            for m in range(32):
                E.wait("t", "v")
                pe = E.t(TRN(PB[5][:, 0:128], (xI if m < 16 else zS)[:, (m % 16) * 128:((m % 16) + 1) * 128], eye[:]), 1)
                E.wait("v", "T", pe)
                E.v(CPY(hbf[:, m * 128:(m + 1) * 128], PB[5][:, 0:128]), 1)
            if c == 0 and tt == 0:
                E.v(CPY(sc1[:, 0:2048], hbf[:, 0:2048]), 1)
                dump("sd_hbf", sc1[:, 0:2048])
                E.v(CPY(sc1[:, 0:2048], hbf[:, 2048:4096]), 1)
                dump("sd_hbf2", sc1[:, 0:2048])
            vv = E.v(CPY(smB[:, 2:3], scqh[:, 0:1]), 1)
            E.wait("t", "v", vv)
            pe = E.t(TRN(PB[4][0:1, 384:512], smB[:, 2:3], eye[:]), 1)
            E.wait("v", "T", pe)
            vv = E.v(CPY(srow[0:1, 0:128], PB[4][0:1, 384:512]), 1)
            E.wait("t", "v", vv)
            pe = E.t(MM(PB[4][:, 256:384], oK1[:], srow[0:1, 0:128]), 1)
            E.wait("v", "T", pe)
            E.v(RECIP(scRep2[:], PB[4][:, 256:384]), 1)
            E.v(TS(scRep2[:], scRep2[:], s2Sq[:, 0:1], ALU.mult), 1)
            E.wait("t", "v")
            for k in range(32):
                nmb, slb = ("wAb", wAb) if k % 2 == 0 else ("wBb", wBb)
                E.wait("y", "t", slab_use[nmb])
                ld = E.y(DMA(slb[:, 0:D], scr["w2qT"][k * 128:(k + 1) * 128, :]), 16)
                E.wait("t", "y", ld)
                for m in range(8):
                    bank = PB[4 + m // 4]
                    off = (m % 4) * 128
                    E.t(MM(bank[:, off:off + 128], slb[:, m * 128:(m + 1) * 128],
                           hbf[:, k * 128:(k + 1) * 128],
                           start=(k == 0 and off == 0), stop=(k == 31)), inc=(1 if m == 7 else 0))
                slab_use[nmb] = E.c["t"]
            E.wait("v", "T")
            if c == 0 and tt == 0:
                E.v(CPY(sc1[:, 0:128], PB[4][:, 0:128]), 1)
                dump("sd_w2raw", sc1[:, 0:128])
                E.v(CPY(sc1[:, 0:128], PB[5][:, 0:128]), 1)
                dump("sd_w2rawB", sc1[:, 0:128])
                dump("sd_screp2t0", scRep2[:])
            if c == 0 and tt == 0 and "sd_w2rx" in dbg:
                for k in range(32):
                    E.wait("y", "T")
                    ld = E.y(DMA(wBb[:, 0:D], scr["w2qT"][k * 128:(k + 1) * 128, :]), 16)
                    E.wait("t", "y", ld)
                    E.t(MM(PB[6][:, 0:128], wBb[:, 0:128], hbf[:, k * 128:(k + 1) * 128],
                           start=(k == 0), stop=(k == 31)), 1)
                E.wait("v", "T")
                E.v(CPY(sc1[:, 128:256], PB[6][:, 0:128]), 1)
                dump("sd_w2rx", sc1[:, 128:256])
            for m in range(8):
                bank = PB[4 + m // 4]
                off = (m % 4) * 128
                E.v(TT(sc2[:, m * T_O + tt * 128:m * T_O + (tt + 1) * 128],
                       bank[:, off:off + 128], scRep2[:], ALU.mult), 1)
        dump(f"bout_{c}", sc2[:])
        dump(f"screp2_{c}", scRep2[:])
        layernorm_fm(sc2, xR, add_into=xR, sqbuf=sc1)

    load_const()
    if stop_after != "const":
        prep(upto=int(stop_after[4:]) if stop_after.startswith("prep") and len(stop_after) > 4 else 9)
    h_pair = [hA, hB]
    order = ["load", "l0_pre", "scan", "l0_post", "l0_ffn", "l1_attn", "l1_ffn", "store"]
    kmax = len(order) if stop_after in ("full",) else (order.index(stop_after) + 1 if stop_after in order else 0)
    if stop_after not in ("const", "prep"):
        for c in range(NCH):
            fns = [lambda: load_x_chunk(c), lambda: l0_pre(c), lambda: scan_chunk(c, h_pair),
                   lambda: l0_post(c), lambda: l0_ffn(c), lambda: l1_attn(c), lambda: l1_ffn(c),
                   lambda: store_x_chunk(c)]
            for fi_ in range(kmax):
                fns[fi_]()
    E.wait("g", "g", E.c["g"])
    E.wait("g", "y", E.c["y"])
    for e in ("v", "a", "T"):
        E.wait("g", e)

    with nc.Block() as block:
        @block.sync
        def _(eng):
            E.replay("y", eng, sems)

        @block.tensor
        def _(eng):
            E.replay("t", eng, sems)

        @block.vector
        def _(eng):
            E.replay("v", eng, sems)

        @block.scalar
        def _(eng):
            E.replay("a", eng, sems)

        @block.gpsimd
        def _(eng):
            E.replay("g", eng, sems)

    st.close()
    return nc, dbg_h


def host_constants():
    c = {}
    c["ones128"] = np.ones((128, 128), np.float32)
    c["eye128"] = np.eye(128, dtype=np.float32)
    c["onesK1"] = np.ones((1, 128), np.float32)
    kk, qq = np.meshgrid(np.arange(128), np.arange(128), indexing="ij")
    c["maskUT"] = (kk <= qq).astype(np.float32)
    phases = np.linspace(0.0, 2.0 * np.pi * (1.0 - 1.0 / NST), NST)
    h0 = np.stack([0.01 * np.cos(phases), 0.01 * np.sin(phases)], -1)
    h0c = np.broadcast_to(h0[None, :, :], (8, NST, 2)).reshape(-1)
    c["h0c"] = np.broadcast_to(h0c.astype(np.float32)[None, :], (128, 256)).copy()
    m2 = (h0[:, 0] ** 2 + h0[:, 1] ** 2)
    m2c0 = np.broadcast_to(m2[None, :], (8, NST)).reshape(-1)
    c["m2c0"] = np.broadcast_to(m2c0.astype(np.float32)[None, :], (128, 128)).copy()
    sh = np.zeros((128, 32), np.float32)
    sh[:, 0::2] = 1.024e-7
    c["hshift"] = sh
    return c


def stage_tiles(wstack, transpose_m):
    out = np.zeros((12, 32, 2, 128, 128), np.float32)
    w_all = np.asarray(wstack, np.float32)
    for s in range(12):
        stp = 2 ** s
        w = w_all[s]
        k = np.arange(2048)
        i = (k // stp) * (2 * stp) + (k % stp)
        j = i + stp
        for (r, cc, v) in ((i, i, w[:, 0, 0]), (i, j, w[:, 0, 1]),
                           (j, i, w[:, 1, 0]), (j, j, w[:, 1, 1])):
            if transpose_m:
                r, cc = cc, r
            mblk = r // 128
            kblk = cc // 128
            slot = (mblk != kblk).astype(np.int64)
            out[s, mblk, slot, cc % 128, r % 128] = v
    return out


def kernel(**inputs):
    c = host_constants()
    bfU = stage_tiles(inputs["L0_ffn_up_weights"], False)
    bfD = stage_tiles(inputs["L0_ffn_down_weights"], True)

    ren = {
        "W_in": "L0_attn_W_in", "W_xproj": "L0_attn_W_xproj", "W_dt": "L0_attn_W_dt",
        "b_dt": "L0_attn_b_dt", "W_out": "L0_attn_W_out", "b_out": "L0_attn_b_out",
        "W_res": "L0_attn_W_res", "A_log": "L0_attn_A_log",
        "W_exp": "L0_ffn_W_exp", "b_exp": "L0_ffn_b_exp", "up_bias": "L0_ffn_up_bias",
        "W_con": "L0_ffn_W_con", "b_con": "L0_ffn_b_con", "down_bias": "L0_ffn_down_bias",
        "ln1_w": "L1_attn_ln1_w", "ln1_b": "L1_attn_ln1_b",
        "q_w": "L1_attn_q_w", "q_b": "L1_attn_q_b", "k_w": "L1_attn_k_w", "k_b": "L1_attn_k_b",
        "v_w": "L1_attn_v_w", "v_b": "L1_attn_v_b", "o_w": "L1_attn_o_w", "o_b": "L1_attn_o_b",
        "ln2_w": "L1_attn_ln2_w", "ln2_b": "L1_attn_ln2_b",
        "f1_w": "L1_attn_f1_w", "f1_b": "L1_attn_f1_b", "f2_w": "L1_attn_f2_w",
        "f2_b": "L1_attn_f2_b", "w1": "L1_ffn_w1", "w2": "L1_ffn_w2",
    }
    base = {k: np.ascontiguousarray(np.asarray(inputs[v], np.float32)) for k, v in ren.items()}
    base.update(c)
    base["bfU"] = bfU
    base["bfD"] = bfD

    nc, _ = build()
    x = np.asarray(inputs["L0_attn_x"], np.float32)
    in_maps = []
    for b in range(8):
        m = dict(base)
        m["xb"] = np.ascontiguousarray(x[b])
        in_maps.append(m)
    res = run_bass_kernel_spmd(nc, in_maps, core_ids=list(range(8)))
    out = np.stack([r["out"] for r in res.results], axis=0)
    return out.astype(np.float32)
